# version 19
# baseline (speedup 1.0000x reference)
import numpy as np
import ml_dtypes
from contextlib import ExitStack

import concourse.bass as bass
import concourse.mybir as mybir
from concourse.bass_utils import run_bass_kernel_spmd

F32 = mybir.dt.float32
BF16 = mybir.dt.bfloat16
I32 = mybir.dt.int32
AF = mybir.ActivationFunctionType
ALU = mybir.AluOpType
AX = mybir.AxisListType
BFNP = ml_dtypes.bfloat16

N_CORES = 8
DEBUG = False
B, N, D, H, DH = 2, 4096, 2048, 16, 128
DC = D // 128
NG = 8
SCALE = float(DH) ** -0.5
EXPB = -8.0
NEG = -1.0e9
EPS = 1e-5
PVOFF = (0, 129, 258, 512)

SEMS = (["s_ld", "s_gp", "cc", "t_sem", "v_sem", "a_sem", "phase", "agst"]
        + [f"x{i}" for i in range(4)] + ["wt0", "wt1"]
        + [f"sl{i}" for i in range(3)] + [f"xr{i}" for i in range(4)] + ["wb0", "wb1", "wu0", "wu1"]
        + [f"y{i}" for i in range(4)] + ["sg0", "sg1", "wob", "g_sem"]
        + ["ws0", "ws1", "wv0", "wv1", "s_sc", "s_ve"])


class W:

    def __init__(self, en, E, S, m):
        self.en, self.E, self.S, self.m = en, E, S, m

    def inc(self, instr, s, k=1):
        self.m[s] += k
        if instr is not None:
            instr.then_inc(self.S[s], k)
        return self.m[s]

    def wait(self, tgt, s, v):
        if self.en == tgt and v > 0:
            self.E.wait_ge(self.S[s], v)

    def run(self, tgt, f, s=None, k=1):
        instr = f(self.E) if self.en == tgt else None
        if s is not None:
            return self.inc(instr, s, k)
        return None

    def mark(self, tgt, s):
        return self.run(tgt, lambda e: e.nop(), s)


def build_nc():
    nc = bass.Bass("TRN2", target_bir_lowering=False)

    x_in = nc.dram_tensor("x", [N, D], F32, kind="ExternalInput")
    xres_in = nc.dram_tensor("xres", [1024, D], F32, kind="ExternalInput")
    wqkv_in = nc.dram_tensor("wqkv", [D, 1536], F32, kind="ExternalInput")
    wout_in = nc.dram_tensor("wout", [D, D], F32, kind="ExternalInput")
    wn_in = nc.dram_tensor("wn", [D, D], F32, kind="ExternalInput")
    bias_in = nc.dram_tensor("bias", [1, D], F32, kind="ExternalInput")
    ident_in = nc.dram_tensor("ident", [128, 128], BF16, kind="ExternalInput")
    masks_in = nc.dram_tensor("masks", [128, 2048], F32, kind="ExternalInput")
    y_out = nc.dram_tensor("y", [1024, D], F32, kind="ExternalOutput")
    if DEBUG:
        dbg_x2 = nc.dram_tensor("dbg_x2", [128, 8 * D], F32, kind="ExternalOutput")
        dbg_h2T = nc.dram_tensor("dbg_h2T", [128, DC * 1024], BF16, kind="ExternalOutput")
        dbg_aoT = nc.dram_tensor("dbg_aoT", [128, DC * 1024], BF16, kind="ExternalOutput")

    ag_ins = [nc.dram_tensor(f"ag_in{g}", [512, 512], BF16, kind="Internal")
              for g in range(NG)]
    ag_outs = [nc.dram_tensor(f"ag_out{g}", [N_CORES * 512, 512], BF16,
                              kind="Internal", addr_space="Shared")
               for g in range(NG)]
    wout_bf = nc.dram_tensor("wout_bf", [D, D], BF16, kind="Internal")
    wn_bf = nc.dram_tensor("wn_bf", [D, D], BF16, kind="Internal")
    sgnN = nc.dram_tensor("sgnN", [D, D], BF16, kind="Internal")

    saved_m = {}

    with ExitStack() as ctx0:
        S = {n: ctx0.enter_context(nc.semaphore(n)) for n in SEMS}
        sb0 = lambda *a: ctx0.enter_context(nc.sbuf_tensor(*a))
        ident = sb0("ident_sb", [128, 128], BF16)
        ones128 = sb0("ones128", [128, 128], BF16)
        biasB = sb0("biasB_sb", [128, D], F32)
        cvec = sb0("cvec", [128, 8], F32)

        with ExitStack() as ctx1:
            sb = lambda *a: ctx1.enter_context(nc.sbuf_tensor(*a))
            psb = lambda *a: ctx1.enter_context(nc.psum_tensor(*a))
            masks = sb("masks_sb", [128, 2048], F32)
            wqkv = sb("wqkv_sb", [128, DC * 1536], BF16)
            xb = sb("xb", [128, 4, D], BF16)
            hb = sb("hb", [128, 4, D], BF16)
            hT = sb("hT", [128, DC * 512], BF16)
            qT = sb("qT", [128, 2 * 2048], BF16)
            kT = sb("kT", [128, 4 * 4096], BF16)
            vext = sb("vext", [128, 4, 32, 129], BF16)
            PT = sb("PT", [128, 2048], BF16)
            ao_st = sb("ao_st", [128, 4, 512], BF16)
            rl = sb("rl", [128, 8], F32)
            lnv = sb("lnv", [128, 12], F32)
            diagR = sb("diagR", [128, 4, 128], BF16)
            ps_st = psb("ps_st", [128, 2048], F32)
            ps_pv = psb("ps_pv", [128, 1024], F32)
            ps_prepA = psb("ps_prepA", [128, 512], F32)
            ps_prepB = psb("ps_prepB", [128, 512], F32)

            def walk1(en, E):
                m = {s: 0 for s in SEMS}
                w = W(en, E, S, m)

                w.run("sync", lambda e: e.dma_start(out=ident[:, :], in_=ident_in.ap()), "s_ld", 16)
                w.run("sync", lambda e: e.dma_start(out=masks[:, :], in_=masks_in.ap()), "s_ld", 16)
                w.run("sync", lambda e: e.dma_start(
                    out=biasB[:, :], in_=bass.AP(bias_in, 0, [[0, 128], [1, D]])), "s_ld", 16)
                ld_const = m["s_ld"]

                w.wait("vector", "s_ld", ld_const)
                w.wait("tensor", "s_ld", ld_const)
                if en == "vector":
                    E.memset(ones128[:, :], 1.0)
                    E.memset(cvec[:, 0:1], SCALE)
                    E.memset(cvec[:, 1:2], EXPB)
                    E.memset(cvec[:, 2:3], 1.0 / D)
                    E.memset(cvec[:, 3:4], EPS)
                    E.memset(cvec[:, 4:5], -0.5)
                    E.memset(bass.AP(vext, 128,
                                     [[4 * 32 * 129, 128], [32 * 129, 4], [129, 32], [1, 1]])
                             , 1.0).then_inc(S["v_sem"], 1)
                m["v_sem"] += 1
                v_const = m["v_sem"]

                xld = [0] * 32
                lnx_done = [0] * 32
                stat_done = [0] * 32
                rstdA_done = [0] * 32
                trp_done = {}
                tr_evac = {}
                qk_mm = {}
                qk_evac = {}
                v_mm = {}
                v_evac = {}
                st_mm = {}
                mask_done = {}
                exp_done = {}
                pv_done = {}
                aoev_done = {}
                ag_store = {}
                st_u = [0]
                ev_u = [0]
                pp = [0]
                prep_evac = {}
                wqld_box = [0]

                def gp_x(k):
                    if k >= 4:
                        w.wait("gpsimd", "v_sem", lnx_done[k - 4])
                    sx = f"x{k % 4}"
                    w.run("gpsimd", lambda e, k=k: e.dma_start(
                        out=xb[:, k % 4, :], in_=x_in.ap()[k * 128:(k + 1) * 128, :]), sx, 16)
                    xld[k] = (sx, m[sx])

                def ln_stats(k):
                    p = k % 2
                    w.wait("vector", xld[k][0], xld[k][1])
                    if k >= 2:
                        w.wait("vector", "a_sem", rstdA_done[k - 2])
                    w.run("vector", lambda e, k=k, p=p: e.reduce_sum(
                        lnv[:, 0 + p:1 + p], xb[:, k % 4, :], axis=AX.X), "v_sem")
                    w.wait("vector", "v_sem", m["v_sem"])
                    w.run("vector", lambda e, p=p: e.tensor_scalar_mul(
                        lnv[:, 2 + p:3 + p], lnv[:, 0 + p:1 + p], -1.0 / D), "v_sem")
                    stat_done[k] = m["v_sem"]

                def ln_sq(k):
                    p = k % 2
                    w.wait("scalar", "v_sem", stat_done[k])
                    if k >= 4:
                        g_, tt_ = divmod(k - 4, 4)
                        w.wait("scalar", "t_sem", trp_done[(g_, tt_, 3)])
                    w.run("scalar", lambda e, k=k, p=p: e.activation(
                        out=hb[:, k % 4, :], in_=xb[:, k % 4, :], func=AF.Square,
                        bias=lnv[:, 2 + p:3 + p], accum_out=lnv[:, 4 + p:5 + p]), "a_sem")
                    w.wait("scalar", "a_sem", m["a_sem"])
                    w.run("scalar", lambda e, p=p: e.activation(
                        out=lnv[:, 6 + p:7 + p], in_=lnv[:, 4 + p:5 + p], func=AF.Ln,
                        scale=cvec[:, 2:3], bias=cvec[:, 3:4]), "a_sem")
                    w.wait("scalar", "a_sem", m["a_sem"])
                    w.run("scalar", lambda e, p=p: e.activation(
                        out=lnv[:, 8 + p:9 + p], in_=lnv[:, 6 + p:7 + p], func=AF.Exp,
                        scale=cvec[:, 4:5]), "a_sem")
                    w.wait("scalar", "a_sem", m["a_sem"])
                    w.run("scalar", lambda e, k=k, p=p: e.activation(
                        out=diagR[:, k % 4, :], in_=ident[:, :], func=AF.Copy,
                        scale=lnv[:, 8 + p:9 + p]), "a_sem")
                    rstdA_done[k] = m["a_sem"]

                def ln_norm(k):
                    p = k % 2
                    w.wait("vector", "a_sem", rstdA_done[k])
                    if k >= 4:
                        g_, tt_ = divmod(k - 4, 4)
                        w.wait("vector", "t_sem", trp_done[(g_, tt_, 3)])
                    w.run("vector", lambda e, k=k, p=p: e.tensor_scalar_add(
                        hb[:, k % 4, :], xb[:, k % 4, :],
                        lnv[:, 2 + p:3 + p]), "v_sem")
                    lnx_done[k] = m["v_sem"]

                def ln_tile(k):
                    ln_stats(k)
                    ln_sq(k)
                    ln_norm(k)

                def prep_acquire():
                    i = pp[0]
                    pp[0] += 1
                    if i >= 2:
                        w.wait("tensor", "v_sem", prep_evac[i - 2])
                    return (ps_prepA if i % 2 == 0 else ps_prepB), i

                def trq(gg, tt, quad):
                    k = 4 * gg + tt
                    pst, i = prep_acquire()
                    w.wait("tensor", "v_sem", lnx_done[k])
                    w.wait("tensor", "a_sem", rstdA_done[k])
                    for j in range(4):
                        dcc = quad * 4 + j
                        w.run("tensor", lambda e, k=k, dcc=dcc, j=j, pst=pst: e.matmul(
                            pst[:, 128 * j:128 * j + 128],
                            hb[:, k % 4, 128 * dcc:128 * dcc + 128],
                            diagR[:, k % 4, :],
                            start=True, stop=True), "t_sem" if j == 3 else None)
                    trp_done[(gg, tt, quad)] = m["t_sem"]
                    w.wait("vector", "t_sem", trp_done[(gg, tt, quad)])
                    w.run("vector", lambda e, tt=tt, quad=quad, pst=pst: e.tensor_copy(
                        bass.AP(hT, (quad * 4) * 512 + 128 * tt,
                                [[DC * 512, 128], [512, 4], [1, 128]]),
                        bass.AP(pst, 0, [[512, 128], [128, 4], [1, 128]])),
                        "v_sem")
                    tr_evac[(gg, tt, quad)] = m["v_sem"]
                    prep_evac[i] = m["v_sem"]

                qk_cur = {}

                def qk_half(gg, cc, half):
                    if half == 0:
                        pst, i = prep_acquire()
                        qk_cur[(gg, cc)] = (pst, i)
                        w.wait("tensor", "v_sem", tr_evac[(gg, 3, 3)])
                    pst, i = qk_cur[(gg, cc)]
                    w.wait("tensor", "s_gp", wqld_box[0][8 * half + 7])
                    for dcc in range(8 * half, 8 * half + 8):
                        w.run("tensor", lambda e, cc=cc, dcc=dcc, pst=pst: e.matmul(
                            pst[:, 0:512],
                            wqkv[:, dcc * 1536 + cc * 128: dcc * 1536 + cc * 128 + 128],
                            hT[:, dcc * 512:dcc * 512 + 512],
                            start=(dcc == 0), stop=(dcc == DC - 1)),
                            "t_sem" if dcc == DC - 1 else None)
                    if half == 1:
                        qk_mm[(gg, cc)] = m["t_sem"]
                        w.wait("vector", "t_sem", qk_mm[(gg, cc)])
                        if cc < 4:
                            w.run("vector", lambda e, cc=cc, gg=gg, pst=pst: e.tensor_copy(
                                qT[:, (gg % 2) * 2048 + cc * 512:(gg % 2) * 2048 + cc * 512 + 512],
                                pst[:, 0:512]), "v_sem")
                        else:
                            w.run("vector", lambda e, cc=cc, gg=gg, pst=pst: e.tensor_copy(
                                kT[:, (cc - 4) * 4096 + gg * 512:(cc - 4) * 4096 + gg * 512 + 512],
                                pst[:, 0:512]), "v_sem")
                        qk_evac[(gg, cc)] = m["v_sem"]
                        prep_evac[i] = m["v_sem"]

                v_cur = {}

                def v_half(gg, tt, half):
                    if half == 0:
                        pst, i = prep_acquire()
                        v_cur[(gg, tt)] = (pst, i)
                        w.wait("tensor", "v_sem", tr_evac[(gg, 3, 3)])
                    pst, i = v_cur[(gg, tt)]
                    w.wait("tensor", "s_gp", wqld_box[0][8 * half + 7])
                    for dcc in range(8 * half, 8 * half + 8):
                        w.run("tensor", lambda e, tt=tt, dcc=dcc, pst=pst: e.matmul(
                            pst[:, 0:512],
                            hT[:, dcc * 512 + 128 * tt: dcc * 512 + 128 * tt + 128],
                            wqkv[:, dcc * 1536 + 1024: dcc * 1536 + 1536],
                            start=(dcc == 0), stop=(dcc == DC - 1)),
                            "t_sem" if dcc == DC - 1 else None)
                    if half == 1:
                        v_mm[(gg, tt)] = m["t_sem"]
                        jc = 4 * gg + tt
                        w.wait("vector", "t_sem", v_mm[(gg, tt)])
                        w.wait("vector", "v_sem", v_const)
                        w.run("vector", lambda e, jc=jc, pst=pst: e.tensor_copy(
                            bass.AP(vext, jc * 129,
                                    [[4 * 32 * 129, 128], [32 * 129, 4], [1, 128]]),
                            bass.AP(pst, 0, [[512, 128], [128, 4], [1, 128]])),
                            "v_sem")
                        v_evac[(gg, tt)] = m["v_sem"]
                        prep_evac[i] = m["v_sem"]

                def prep_items(gg):
                    its = [(lambda tt=tt: ln_tile(4 * gg + tt)) for tt in range(4)]
                    if gg + 1 < NG:
                        its += [(lambda tt=tt: gp_x(4 * (gg + 1) + tt))
                                for tt in range(4)]
                    for tt in range(4):
                        for quad in range(4):
                            its.append(lambda tt=tt, quad=quad: trq(gg, tt, quad))
                    for cc in range(8):
                        for half in range(2):
                            its.append(lambda cc=cc, half=half: qk_half(gg, cc, half))
                    for tt in range(4):
                        for half in range(2):
                            its.append(lambda tt=tt, half=half: v_half(gg, tt, half))
                    return its

                wq_vals = []

                def wq_chunk(dcc):
                    w.run("gpsimd", lambda e, dcc=dcc: e.dma_start(
                        out=wqkv[:, dcc * 1536:(dcc + 1) * 1536],
                        in_=wqkv_in.ap()[dcc * 128:(dcc + 1) * 128, :]), "s_gp", 16)
                    wq_vals.append(m["s_gp"])

                gp_x(0)
                gp_x(1)
                wq_chunk(0)
                wq_chunk(1)
                gp_x(2)
                gp_x(3)
                for dcc in range(2, DC):
                    wq_chunk(dcc)
                wqld_box[0] = wq_vals

                for it in prep_items(0):
                    it()

                for g in range(NG):
                    items = prep_items(g + 1) if g + 1 < NG else []

                    for half in range(2):
                        dcc = 2 * g + half
                        w.run("gpsimd", lambda e, dcc=dcc: e.dma_start(
                            out=wout_bf.ap()[128 * dcc:128 * dcc + 128, :],
                            in_=wout_in.ap()[128 * dcc:128 * dcc + 128, :]), "wob", 16)
                        w.run("gpsimd", lambda e, dcc=dcc: e.dma_start(
                            out=wn_bf.ap()[128 * dcc:128 * dcc + 128, :],
                            in_=wn_in.ap()[128 * dcc:128 * dcc + 128, :]), "g_sem", 16)

                    units_g = []
                    for h in range(4):
                        for g2 in (2 * g, 2 * g + 1):
                            nchunk = 2 * g2 + 2
                            uls = []
                            jc0 = 0
                            if nchunk % 4 == 2:
                                uls.append((0, 2))
                                jc0 = 2
                            while jc0 < nchunk:
                                uls.append((jc0, 4))
                                jc0 += 4
                            for idx, (jc0_, wd_) in enumerate(uls):
                                units_g.append(dict(
                                    h=h, g2=g2, jc0=jc0_, wd=wd_,
                                    last=(idx == len(uls) - 1)))

                    def emit_st_exp(ud):
                        h, g2, jc0, wd = ud["h"], ud["g2"], ud["jc0"], ud["wd"]
                        u = st_u[0]
                        st_u[0] += 1
                        ud["u"] = u
                        base = 1024 * (u % 2)
                        qoff = (g % 2) * 2048 + h * 512 + 256 * (g2 % 2)
                        if u >= 2:
                            w.wait("tensor", "a_sem", exp_done[u - 2])
                        for jj in range(wd):
                            jc = jc0 + jj
                            sl = base + 256 * jj
                            w.wait("tensor", "v_sem", qk_evac[(jc // 4, 4 + h)])
                            w.wait("tensor", "v_sem", qk_evac[(g, h)])
                            w.run("tensor", lambda e, sl=sl, jc=jc, h=h, qoff=qoff: e.matmul(
                                ps_st[:, sl:sl + 256],
                                kT[:, h * 4096 + jc * 128: h * 4096 + jc * 128 + 128],
                                qT[:, qoff:qoff + 256],
                                start=True, stop=True),
                                "t_sem" if jj == wd - 1 else None)
                        st_mm[u] = m["t_sem"]

                        need_mask = (jc0 + wd - 1) >= 2 * g2
                        if need_mask:
                            w.wait("vector", "t_sem", st_mm[u])
                            dds = [jc0 + jj - 2 * g2 for jj in range(wd)]
                            dds = [(jj, d_) for jj, d_ in enumerate(dds) if d_ >= 0]
                            for ii_, (jj, dd) in enumerate(dds):
                                sl = base + 256 * jj
                                w.run("vector", lambda e, sl=sl, dd=dd: e.tensor_tensor(
                                    ps_st[:, sl:sl + 256], ps_st[:, sl:sl + 256],
                                    masks[:, dd * 512:dd * 512 + 256], ALU.add),
                                    "v_sem" if ii_ == len(dds) - 1 else None)
                            mask_done[u] = m["v_sem"]

                        w.wait("scalar", "t_sem", st_mm[u])
                        if need_mask:
                            w.wait("scalar", "v_sem", mask_done[u])
                        if u >= 2:
                            w.wait("scalar", "t_sem", pv_done[u - 2])
                        w.run("scalar", lambda e, base=base, u=u, wd=wd: e.activation(
                            out=PT[:, (u % 2) * 1024:(u % 2) * 1024 + 256 * wd],
                            in_=ps_st[:, base:base + 256 * wd], func=AF.Exp,
                            scale=cvec[:, 0:1], bias=cvec[:, 1:2]), "a_sem")
                        exp_done[u] = m["a_sem"]

                    def emit_pv(ud):
                        h, g2, jc0, wd, u = ud["h"], ud["g2"], ud["jc0"], ud["wd"], ud["u"]
                        w.wait("tensor", "a_sem", exp_done[u])
                        if jc0 == 0:
                            w.wait("tensor", "v_sem", v_evac[(g, 3)])
                            uu = ev_u[0]
                            if uu > 0:
                                w.wait("tensor", "v_sem", aoev_done[uu - 1])
                            w.wait("tensor", "v_sem", v_const)
                        for jj in range(wd):
                            jc = jc0 + jj
                            for isub in range(2):
                                off = 512 * isub
                                last = (jj == wd - 1 and isub == 1)
                                w.run("tensor",
                                      lambda e, jc=jc, off=off, u=u, jj=jj, isub=isub, h=h, g2=g2: e.matmul(
                                          ps_pv[:, off:off + 129],
                                          PT[:, (u % 2) * 1024 + 256 * jj + 128 * isub:
                                             (u % 2) * 1024 + 256 * jj + 128 * isub + 128],
                                          vext[:, h, jc, :],
                                          start=(jc == 0), stop=(jc == 2 * g2 + 1)),
                                      "t_sem" if last else None)
                        pv_done[u] = m["t_sem"]

                        if ud["last"]:
                            uu = ev_u[0]
                            ev_u[0] += 1
                            rbase = 2 * (uu % 4)
                            w.wait("vector", "t_sem", pv_done[u])
                            if g > 0 and h == 0 and g2 == 2 * g:
                                w.wait("vector", "agst", ag_store[(g - 1, 3)])
                            for isub in range(2):
                                off = 512 * isub
                                isg = 2 * (g2 % 2) + isub
                                w.run("vector", lambda e, isub=isub, off=off, rbase=rbase: e.reciprocal(
                                    rl[:, rbase + isub:rbase + isub + 1],
                                    ps_pv[:, off + 128:off + 129]), "v_sem")
                                w.wait("vector", "v_sem", m["v_sem"])
                                w.run("vector", lambda e, isub=isub, off=off, h=h, isg=isg, rbase=rbase: e.tensor_scalar_mul(
                                    ao_st[:, isg, h * 128:h * 128 + 128],
                                    ps_pv[:, off:off + 128],
                                    rl[:, rbase + isub:rbase + isub + 1]), "v_sem")
                            aoev_done[uu] = m["v_sem"]

                    pend = None
                    n_items = len(items)
                    nU = len(units_g)
                    done = 0
                    for k_u, ud in enumerate(units_g):
                        emit_st_exp(ud)
                        want = (k_u + 1) * n_items // nU
                        while done < want:
                            items[done]()
                            done += 1
                        if pend is not None:
                            emit_pv(pend)
                        pend = ud
                    emit_pv(pend)
                    while done < n_items:
                        items[done]()
                        done += 1

                    for isub in range(4):
                        w.wait("sync", "v_sem", aoev_done[ev_u[0] - 1])
                        w.run("sync", lambda e, g=g, isub=isub: e.dma_start(
                            out=ag_ins[g].ap()[isub * 128:isub * 128 + 128, :],
                            in_=ao_st[:, isub, :]), "agst", 16)
                        ag_store[(g, isub)] = m["agst"]

                    w.wait("gpsimd", "agst", ag_store[(g, 3)])
                    w.run("gpsimd", lambda e, g=g: e.collective_compute(
                        "AllGather", ALU.bypass,
                        replica_groups=[list(range(N_CORES))],
                        ins=[ag_ins[g].ap()],
                        outs=[ag_outs[g].ap()]), "cc", 1)


                if en == "sync":
                    E.wait_ge(S["s_ld"], m["s_ld"])
                    E.wait_ge(S["agst"], m["agst"])
                    for sx in ("sg0", "sg1"):
                        if m[sx] > 0:
                            E.wait_ge(S[sx], m[sx])
                if en == "gpsimd":
                    E.wait_ge(S["s_gp"], m["s_gp"])
                    E.wait_ge(S["g_sem"], m["g_sem"])
                    for sx in ("x0", "x1", "x2", "x3", "wt0", "wt1", "wob"):
                        if m[sx] > 0:
                            E.wait_ge(S[sx], m[sx])
                if en == "tensor":
                    E.wait_ge(S["t_sem"], m["t_sem"])
                if en == "vector":
                    E.wait_ge(S["v_sem"], m["v_sem"])
                    if m["s_ve"] > 0:
                        E.wait_ge(S["s_ve"], m["s_ve"])
                if en == "scalar":
                    E.wait_ge(S["a_sem"], m["a_sem"])
                    if m["s_sc"] > 0:
                        E.wait_ge(S["s_sc"], m["s_sc"])
                w.mark(en, "phase")
                m["phase"] = 5
                saved_m[en] = m

            with nc.Block() as blk:
                blk.sync(lambda e: walk1("sync", e))
                blk.gpsimd(lambda e: walk1("gpsimd", e))
                blk.scalar(lambda e: walk1("scalar", e))
                blk.vector(lambda e: walk1("vector", e))
                blk.tensor(lambda e: walk1("tensor", e))

        with ExitStack() as ctx2:
            sb = lambda *a: ctx2.enter_context(nc.sbuf_tensor(*a))
            psb = lambda *a: ctx2.enter_context(nc.psum_tensor(*a))
            pool = sb("pool", [128, 3, D], BF16)
            aoT = sb("aoT", [128, DC * 1024], BF16)
            wblk = sb("wblk", [128, 2, DC * 512], BF16)
            x2 = sb("x2", [128, 8, D], F32)
            h2T = sb("h2T", [128, DC * 1024], BF16)
            ystg = sb("ystg", [128, 4, 512], F32)
            lnv2 = sb("lnv2", [128, 12], F32)
            wnb = sb("wnb", [128, 2, D], BF16)
            acol = sb("acol", [128, 4], F32)
            ps_ep = psb("ps_ep", [128, 2048], F32)
            ps_f = psb("ps_f", [128, 1024], F32)
            ps_t2A = psb("ps_t2A", [128, 1024], BF16)
            ps_t2B = psb("ps_t2B", [128, 1024], BF16)

            def walk2(en, E):
                m = saved_m[en]
                w = W(en, E, S, m)
                w.wait(en, "phase", 5)

                aold = {}
                trp2 = {}
                aoT_done = [0] * 8
                opmm = {}
                x2_done = {}
                stat2 = [0] * 8
                rstd2 = [0] * 8
                norm2 = [0] * 8
                x2b_done = [0] * 8
                h2tr = {}
                h2T_done = [0] * 8
                wld = {}
                ffnmm = {}
                y_done = {}
                ystore_seq = {}
                sg_cnt = [0] * DC
                ep_last = {}
                f_last = {}
                op_seq = [0]
                y_seq = [0]
                t2p = [0]
                t2_ev = {}

                idxreg = [None]
                if en == "sync":
                    idxreg[0] = E.partition_id()

                def t2_acquire():
                    i = t2p[0]
                    t2p[0] += 1
                    if i >= 2:
                        w.wait("tensor", "v_sem", t2_ev[i - 2])
                    return (ps_t2A if i % 2 == 0 else ps_t2B), i

                wt_ld = [0] * DC

                def sgn_load(oc):
                    p = oc % 2
                    if oc >= 2:
                        w.wait("gpsimd", f"sg{p}", sg_cnt[oc - 2])
                    w.run("gpsimd", lambda e, oc=oc, p=p: e.dma_start(
                        out=wnb[:, p, :],
                        in_=wn_bf.ap()[oc * 128:oc * 128 + 128, :]), f"wt{p}", 16)
                    wt_ld[oc] = m[f"wt{p}"]

                def sgn_math(oc):
                    p = oc % 2
                    w.wait("vector", f"wt{p}", wt_ld[oc])
                    w.run("vector", lambda e, p=p: e.reduce_sum(
                        acol[:, p:p + 1], wnb[:, p, :], axis=AX.X,
                        apply_absolute_value=True), "v_sem")
                    w.wait("vector", "v_sem", m["v_sem"])
                    w.run("vector", lambda e, p=p: e.tensor_scalar_mul(
                        acol[:, 2 + p:3 + p], acol[:, p:p + 1], 1.0 / D), "v_sem")
                    red_done = m["v_sem"]
                    w.wait("scalar", "v_sem", red_done)
                    w.run("scalar", lambda e, p=p: e.activation(
                        out=wnb[:, p, :], in_=wnb[:, p, :], func=AF.Sign), "a_sem")
                    w.wait("vector", "a_sem", m["a_sem"])
                    w.run("vector", lambda e, p=p: e.tensor_scalar_mul(
                        wnb[:, p, :], wnb[:, p, :], acol[:, 2 + p:3 + p]), "v_sem")
                    w.wait("gpsimd", "v_sem", m["v_sem"])
                    w.run("gpsimd", lambda e, oc=oc, p=p: e.dma_start(
                        out=sgnN.ap()[oc * 128:oc * 128 + 128, :],
                        in_=wnb[:, p, :]), f"sg{p}", 16)
                    sg_cnt[oc] = m[f"sg{p}"]

                def wblk_load_w(og):
                    free = {2: (0, 7), 3: (1, 7)}.get(og)
                    eng2 = "gpsimd" if og < 2 else "sync"
                    for eng_l, sem_l, h in (("scalar", f"ws{og % 2}", 0),
                                            (eng2, f"wv{og % 2}", 1)):
                        if free is not None:
                            w.wait(eng_l, "t_sem", opmm[free])
                        w.run(eng_l, lambda e, og=og, h=h: e.dma_start(
                            out=wblk[:, og % 2, h * 8 * 512:(h + 1) * 8 * 512],
                            in_=bass.AP(wout_bf, og * 512 + h * 8 * 128 * D,
                                        [[D, 128], [128 * D, 8], [1, 512]])), sem_l, 16)
                    wld[og] = [(f"ws{og % 2}", m[f"ws{og % 2}"]),
                               (f"wv{og % 2}", m[f"wv{og % 2}"])]

                def wblk_load_f(og):
                    i = 4 + og
                    free = opmm[(og + 2, 7)] if og < 2 else ffnmm[(og - 2, 7)]
                    w.wait("scalar", "t_sem", free)
                    for oc in range(4 * og, 4 * og + 4):
                        w.wait("scalar", f"sg{oc % 2}", sg_cnt[oc])
                    for ic in range(DC):
                        w.run("scalar", lambda e, og=og, ic=ic: e.dma_start_transpose(
                            out=wblk[:, og % 2, ic * 512:ic * 512 + 512],
                            in_=sgnN.ap()[og * 512:og * 512 + 512,
                                          ic * 128:ic * 128 + 128]), f"wu{og % 2}", 16)
                    wld[i] = [(f"wu{og % 2}", m[f"wu{og % 2}"])]

                def slab_load(tt):
                    k = tt // 2
                    b = tt % 2
                    slot = tt % 3
                    ssl = f"sl{slot}"
                    if tt >= 3:
                        w.wait("sync", "t_sem", trp2[(tt - 3, 3)])
                    if en == "sync":
                        for ci in nc.Switch(engines=[E], index=[idxreg[0]], n=N_CORES):
                            g_src = 2 * k + ci // 4
                            E.wait_ge(S["cc"], g_src + 1)
                            E.dma_start(
                                out=bass.AP(pool, slot * 2048,
                                            [[3 * 2048, 128], [512, 4], [1, 512]]),
                                in_=bass.AP(ag_outs[g_src],
                                            (2048 * b + 128 * (ci % 4)) * 512,
                                            [[512, 128], [512 * 512, 4], [1, 512]]),
                            ).then_inc(S[ssl], 16)
                    m[ssl] += 16
                    aold[tt] = (ssl, m[ssl])

                def ao_quads(tt):
                    for quad in range(4):
                        pst, i = t2_acquire()
                        w.wait("tensor", aold[tt][0], aold[tt][1])
                        for j in range(4):
                            cc = quad * 4 + j
                            w.run("tensor", lambda e, tt=tt, cc=cc, j=j, pst=pst: e.transpose(
                                pst[:, 128 * j:128 * j + 128],
                                pool[:, tt % 3, 128 * cc:128 * cc + 128],
                                ident[:, :]), "t_sem" if j == 3 else None)
                        trp2[(tt, quad)] = m["t_sem"]
                        w.wait("vector", "t_sem", trp2[(tt, quad)])
                        w.run("vector", lambda e, tt=tt, quad=quad, pst=pst: e.tensor_copy(
                            bass.AP(aoT, (quad * 4) * 1024 + 128 * tt,
                                    [[DC * 1024, 128], [1024, 4], [1, 128]]),
                            bass.AP(pst, 0, [[1024, 128], [128, 4], [1, 128]])),
                            "v_sem")
                        t2_ev[i] = m["v_sem"]
                    aoT_done[tt] = m["v_sem"]

                def ln2_stats(tt):
                    p = tt % 2
                    if tt >= 2:
                        w.wait("vector", "a_sem", rstd2[tt - 2])
                    w.wait("vector", "v_sem", x2_done[(3, tt)])
                    w.run("vector", lambda e, tt=tt, p=p: e.reduce_sum(
                        lnv2[:, 0 + p:1 + p], x2[:, tt, :], axis=AX.X), "v_sem")
                    w.wait("vector", "v_sem", m["v_sem"])
                    w.run("vector", lambda e, p=p: e.tensor_scalar_mul(
                        lnv2[:, 2 + p:3 + p], lnv2[:, 0 + p:1 + p], -1.0 / D), "v_sem")
                    stat2[tt] = m["v_sem"]

                def ln2_sq(tt):
                    p = tt % 2
                    w.wait("scalar", "v_sem", stat2[tt])
                    w.wait("scalar", "t_sem", trp2[({0: 6, 1: 7, 2: 5}[tt % 3], 3)])
                    if tt >= 3:
                        w.wait("scalar", "t_sem", h2tr[(tt - 3, 3)])
                    w.run("scalar", lambda e, tt=tt, p=p: e.activation(
                        out=pool[:, tt % 3, :], in_=x2[:, tt, :], func=AF.Square,
                        bias=lnv2[:, 2 + p:3 + p], accum_out=lnv2[:, 4 + p:5 + p]), "a_sem")
                    w.wait("scalar", "a_sem", m["a_sem"])
                    w.run("scalar", lambda e, p=p: e.activation(
                        out=lnv2[:, 6 + p:7 + p], in_=lnv2[:, 4 + p:5 + p], func=AF.Ln,
                        scale=cvec[:, 2:3], bias=cvec[:, 3:4]), "a_sem")
                    w.wait("scalar", "a_sem", m["a_sem"])
                    w.run("scalar", lambda e, p=p: e.activation(
                        out=lnv2[:, 8 + p:9 + p], in_=lnv2[:, 6 + p:7 + p], func=AF.Exp,
                        scale=cvec[:, 4:5]), "a_sem")
                    rstd2[tt] = m["a_sem"]

                def ln2_nmr_norm(tt):
                    p = tt % 2
                    w.wait("vector", "a_sem", rstd2[tt])
                    w.run("vector", lambda e, p=p: e.tensor_mul(
                        lnv2[:, 10 + p:11 + p], lnv2[:, 2 + p:3 + p], lnv2[:, 8 + p:9 + p]),
                        "v_sem")
                    w.wait("vector", "v_sem", m["v_sem"])
                    w.run("vector", lambda e, tt=tt, p=p: e.tensor_scalar(
                        pool[:, tt % 3, :], x2[:, tt, :],
                        lnv2[:, 8 + p:9 + p], lnv2[:, 10 + p:11 + p],
                        ALU.mult, ALU.add), "v_sem")
                    norm2[tt] = m["v_sem"]
                    w.run("vector", lambda e, tt=tt: e.tensor_tensor(
                        x2[:, tt, :], x2[:, tt, :], biasB[:, :], ALU.add), "v_sem")
                    x2b_done[tt] = m["v_sem"]

                def h2_quads(tt):
                    for quad in range(4):
                        pst, i = t2_acquire()
                        w.wait("tensor", "v_sem", norm2[tt])
                        for j in range(4):
                            ic = quad * 4 + j
                            w.run("tensor", lambda e, tt=tt, ic=ic, j=j, pst=pst: e.transpose(
                                pst[:, 128 * j:128 * j + 128],
                                pool[:, tt % 3, 128 * ic:128 * ic + 128],
                                ident[:, :]), "t_sem" if j == 3 else None)
                        h2tr[(tt, quad)] = m["t_sem"]
                        w.wait("vector", "t_sem", h2tr[(tt, quad)])
                        w.run("vector", lambda e, tt=tt, quad=quad, pst=pst: e.tensor_copy(
                            bass.AP(h2T, (quad * 4) * 1024 + 128 * tt,
                                    [[DC * 1024, 128], [1024, 4], [1, 128]]),
                            bass.AP(pst, 0, [[1024, 128], [128, 4], [1, 128]])),
                            "v_sem")
                        t2_ev[i] = m["v_sem"]
                    h2T_done[tt] = m["v_sem"]

                def op_tile(og, tt):
                    sl_idx = 2 * (og % 2) + tt % 2
                    sl = 512 * sl_idx
                    prev = ep_last.get(sl_idx)
                    if prev is not None:
                        w.wait("tensor", "v_sem", x2_done[prev])
                    for s_, v_ in wld[og]:
                        w.wait("tensor", s_, v_)
                    w.wait("tensor", "v_sem", aoT_done[tt])
                    for cc in range(DC):
                        w.run("tensor", lambda e, sl=sl, cc=cc, tt=tt, og=og: e.matmul(
                            ps_ep[:, sl:sl + 512],
                            aoT[:, cc * 1024 + 128 * tt: cc * 1024 + 128 * tt + 128],
                            wblk[:, og % 2, 512 * cc:512 * cc + 512],
                            start=(cc == 0), stop=(cc == DC - 1)),
                            "t_sem" if cc == DC - 1 else None)
                    opmm[(og, tt)] = m["t_sem"]
                    ep_last[sl_idx] = (og, tt)
                    w.wait("vector", "t_sem", opmm[(og, tt)])
                    w.wait("vector", f"xr{tt % 4}", xr_ld[tt])
                    w.run("vector", lambda e, sl=sl, og=og, tt=tt: e.tensor_tensor(
                        x2[:, tt, 512 * og:512 * og + 512], ps_ep[:, sl:sl + 512],
                        x2[:, tt, 512 * og:512 * og + 512], ALU.add), "v_sem")
                    x2_done[(og, tt)] = m["v_sem"]

                def ffn_tile(og, tt):
                    sl_idx = tt % 2
                    sl = 512 * sl_idx
                    prev = f_last.get(sl_idx)
                    if prev is not None:
                        w.wait("tensor", "v_sem", y_done[prev])
                    for s_, v_ in wld[4 + og]:
                        w.wait("tensor", s_, v_)
                    w.wait("tensor", "v_sem", h2T_done[tt])
                    for ic in range(DC):
                        w.run("tensor", lambda e, sl=sl, ic=ic, tt=tt, og=og: e.matmul(
                            ps_f[:, sl:sl + 512],
                            h2T[:, ic * 1024 + 128 * tt: ic * 1024 + 128 * tt + 128],
                            wblk[:, og % 2, 512 * ic:512 * ic + 512],
                            start=(ic == 0), stop=(ic == DC - 1)),
                            "t_sem" if ic == DC - 1 else None)
                    ffnmm[(og, tt)] = m["t_sem"]
                    f_last[sl_idx] = (og, tt)
                    kk = y_seq[0]
                    y_seq[0] += 1
                    w.wait("vector", "t_sem", ffnmm[(og, tt)])
                    w.wait("vector", "v_sem", x2b_done[tt])
                    if kk >= 4:
                        w.wait("vector", ystore_seq[kk - 4][0], ystore_seq[kk - 4][1])
                    w.run("vector", lambda e, sl=sl, og=og, tt=tt, kk=kk: e.tensor_tensor(
                        ystg[:, kk % 4, :], ps_f[:, sl:sl + 512],
                        x2[:, tt, 512 * og:512 * og + 512], ALU.add), "v_sem")
                    y_done[(og, tt)] = m["v_sem"]
                    sy = f"y{kk % 4}"
                    w.wait("sync", "v_sem", y_done[(og, tt)])
                    w.run("sync", lambda e, og=og, tt=tt, kk=kk: e.dma_start(
                        out=y_out.ap()[tt * 128:tt * 128 + 128, og * 512:og * 512 + 512],
                        in_=ystg[:, kk % 4, :]), sy, 16)
                    ystore_seq[kk] = (sy, m[sy])

                wblk_load_w(0)
                wblk_load_w(1)

                xr_ld = [0] * 8

                def xr_load(tt):
                    eng_x = "scalar" if tt % 2 == 0 else "sync"
                    w.run(eng_x, lambda e, tt=tt: e.dma_start(
                        out=x2[:, tt, :],
                        in_=xres_in.ap()[tt * 128:tt * 128 + 128, :]),
                        f"xr{tt % 4}", 16)
                    xr_ld[tt] = m[f"xr{tt % 4}"]

                nx_l = [0]
                nx_m = [0]

                def sgn_step():
                    if nx_m[0] < nx_l[0] and nx_m[0] < DC:
                        sgn_math(nx_m[0])
                        nx_m[0] += 1
                    if nx_l[0] < DC and nx_l[0] <= nx_m[0] + 1:
                        sgn_load(nx_l[0])
                        nx_l[0] += 1

                sgn_step()
                for tt in range(6):
                    slab_load(tt)
                    xr_load(tt)
                    ao_quads(tt)
                    sgn_step()
                    op_tile(0, tt)
                for tt in range(6):
                    sgn_step()
                    op_tile(1, tt)
                for tt in (6, 7):
                    slab_load(tt)
                    xr_load(tt)
                    ao_quads(tt)
                    sgn_step()
                    op_tile(0, tt)
                for tt in (6, 7):
                    sgn_step()
                    op_tile(1, tt)
                wblk_load_w(2)
                wblk_load_w(3)
                for tt in range(8):
                    sgn_step()
                    sgn_step()
                    op_tile(2, tt)
                for tt in range(8):
                    op_tile(3, tt)
                    ln2_stats(tt)
                    ln2_sq(tt)
                    ln2_nmr_norm(tt)
                    if tt >= 2:
                        h2_quads(tt - 2)
                h2_quads(6)
                h2_quads(7)
                wblk_load_f(0)
                wblk_load_f(1)
                for og in range(4):
                    for tt in range(8):
                        ffn_tile(og, tt)
                    if og + 2 < 4:
                        wblk_load_f(og + 2)

                if en == "sync":
                    for sx in ("s_ld", "sl0", "sl1", "sl2",
                               "wv0", "wv1", "y0", "y1", "y2", "y3",
                               "xr0", "xr1", "xr2", "xr3"):
                        if m[sx] > 0:
                            E.wait_ge(S[sx], m[sx])
                if en == "gpsimd":
                    for sx in ("wt0", "wt1", "sg0", "sg1"):
                        if m[sx] > 0:
                            E.wait_ge(S[sx], m[sx])
                if en == "scalar":
                    for sx in ("ws0", "ws1", "wu0", "wu1"):
                        if m[sx] > 0:
                            E.wait_ge(S[sx], m[sx])

            with nc.Block() as blk2:
                blk2.sync(lambda e: walk2("sync", e))
                blk2.gpsimd(lambda e: walk2("gpsimd", e))
                blk2.scalar(lambda e: walk2("scalar", e))
                blk2.vector(lambda e: walk2("vector", e))
                blk2.tensor(lambda e: walk2("tensor", e))

    return nc


_NC_CACHE = None


def _get_nc():
    global _NC_CACHE
    if _NC_CACHE is None:
        _NC_CACHE = build_nc()
    return _NC_CACHE


def kernel(L0_attn_x, L0_attn_w_qkv, L0_attn_w_out, L0_ffn_weight, L0_ffn_bias):
    x = np.ascontiguousarray(np.asarray(L0_attn_x, dtype=np.float32))
    wqkv = np.asarray(L0_attn_w_qkv, dtype=np.float32)
    wout = np.ascontiguousarray(np.asarray(L0_attn_w_out, dtype=np.float32))
    wn = np.ascontiguousarray(np.asarray(L0_ffn_weight, dtype=np.float32))
    bias = np.ascontiguousarray(np.asarray(L0_ffn_bias, dtype=np.float32).reshape(1, D))
    ident = np.eye(128, dtype=BFNP)

    jj = np.arange(128)[:, None]
    ii = np.arange(512)[None, :]
    masks = np.concatenate(
        [np.where(ii >= jj + 128 * dd, 0.0, NEG).astype(np.float32) for dd in range(4)],
        axis=1)

    in_maps = []
    for c in range(N_CORES):
        b, g4 = divmod(c, 4)
        wq_sl = np.ascontiguousarray(np.concatenate(
            [wqkv[:, 512 * g4:512 * g4 + 512],
             wqkv[:, 2048 + 512 * g4:2048 + 512 * g4 + 512],
             wqkv[:, 4096 + 512 * g4:4096 + 512 * g4 + 512]], axis=1))
        xres_c = np.concatenate(
            [x[tt % 2, 1024 * (tt // 2) + 128 * c: 1024 * (tt // 2) + 128 * (c + 1)]
             for tt in range(8)], axis=0)
        in_maps.append({
            "x": np.ascontiguousarray(x[b]),
            "xres": np.ascontiguousarray(xres_c),
            "wqkv": wq_sl,
            "wout": wout,
            "wn": wn,
            "bias": bias,
            "ident": ident,
            "masks": masks,
        })

    nc = _get_nc()
    res = run_bass_kernel_spmd(nc, in_maps, core_ids=list(range(N_CORES)))

    out = np.empty((B, N, D), dtype=np.float32)
    for c in range(N_CORES):
        yc = res.results[c]["y"]
        for tt in range(8):
            k, bb = tt // 2, tt % 2
            out[bb, 1024 * k + 128 * c: 1024 * k + 128 * (c + 1)] = \
                yc[tt * 128:(tt + 1) * 128]
    return out


# version 22
# speedup vs baseline: 1.0233x; 1.0233x over previous
import numpy as np
import ml_dtypes
from contextlib import ExitStack

import concourse.bass as bass
import concourse.mybir as mybir
from concourse.bass_utils import run_bass_kernel_spmd

F32 = mybir.dt.float32
BF16 = mybir.dt.bfloat16
I32 = mybir.dt.int32
AF = mybir.ActivationFunctionType
ALU = mybir.AluOpType
AX = mybir.AxisListType
BFNP = ml_dtypes.bfloat16

N_CORES = 8
DEBUG = False
B, N, D, H, DH = 2, 4096, 2048, 16, 128
DC = D // 128
NG = 8
SCALE = float(DH) ** -0.5
EXPB = -8.0
NEG = -1.0e9
EPS = 1e-5
PVOFF = (0, 129, 258, 512)

SEMS = (["s_ld", "s_gp", "cc", "t_sem", "v_sem", "a_sem", "phase", "agst"]
        + [f"x{i}" for i in range(4)] + ["wt0", "wt1"]
        + [f"sl{i}" for i in range(3)] + [f"xr{i}" for i in range(4)] + ["wb0", "wb1", "wu0", "wu1"]
        + [f"y{i}" for i in range(4)] + ["sg0", "sg1", "wob", "g_sem"]
        + ["ws0", "ws1", "wv0", "wv1", "s_sc", "s_ve"])


class W:

    def __init__(self, en, E, S, m):
        self.en, self.E, self.S, self.m = en, E, S, m

    def inc(self, instr, s, k=1):
        self.m[s] += k
        if instr is not None:
            instr.then_inc(self.S[s], k)
        return self.m[s]

    def wait(self, tgt, s, v):
        if self.en == tgt and v > 0:
            self.E.wait_ge(self.S[s], v)

    def run(self, tgt, f, s=None, k=1):
        instr = f(self.E) if self.en == tgt else None
        if s is not None:
            return self.inc(instr, s, k)
        return None

    def mark(self, tgt, s):
        return self.run(tgt, lambda e: e.nop(), s)


def build_nc():
    nc = bass.Bass("TRN2", target_bir_lowering=False)

    x_in = nc.dram_tensor("x", [N, D], F32, kind="ExternalInput")
    xres_in = nc.dram_tensor("xres", [1024, D], F32, kind="ExternalInput")
    wqkv_in = nc.dram_tensor("wqkv", [D, 1536], F32, kind="ExternalInput")
    wout_in = nc.dram_tensor("wout", [D, D], F32, kind="ExternalInput")
    wn_in = nc.dram_tensor("wn", [D, D], F32, kind="ExternalInput")
    bias_in = nc.dram_tensor("bias", [1, D], F32, kind="ExternalInput")
    ident_in = nc.dram_tensor("ident", [128, 128], BF16, kind="ExternalInput")
    masks_in = nc.dram_tensor("masks", [128, 2048], F32, kind="ExternalInput")
    y_out = nc.dram_tensor("y", [1024, D], F32, kind="ExternalOutput")
    if DEBUG:
        dbg_x2 = nc.dram_tensor("dbg_x2", [128, 8 * D], F32, kind="ExternalOutput")
        dbg_h2T = nc.dram_tensor("dbg_h2T", [128, DC * 1024], BF16, kind="ExternalOutput")
        dbg_aoT = nc.dram_tensor("dbg_aoT", [128, DC * 1024], BF16, kind="ExternalOutput")

    ag_ins = [nc.dram_tensor(f"ag_in{k}", [1024, 512], BF16, kind="Internal")
              for k in range(4)]
    ag_outs = [nc.dram_tensor(f"ag_out{k}", [N_CORES * 1024, 512], BF16,
                              kind="Internal", addr_space="Shared")
               for k in range(4)]
    wout_bf = nc.dram_tensor("wout_bf", [D, D], BF16, kind="Internal")
    wn_bf = nc.dram_tensor("wn_bf", [D, D], BF16, kind="Internal")
    sgnN = nc.dram_tensor("sgnN", [D, D], BF16, kind="Internal")

    saved_m = {}

    with ExitStack() as ctx0:
        S = {n: ctx0.enter_context(nc.semaphore(n)) for n in SEMS}
        sb0 = lambda *a: ctx0.enter_context(nc.sbuf_tensor(*a))
        ident = sb0("ident_sb", [128, 128], BF16)
        ones128 = sb0("ones128", [128, 128], BF16)
        biasB = sb0("biasB_sb", [128, D], F32)
        cvec = sb0("cvec", [128, 8], F32)

        with ExitStack() as ctx1:
            sb = lambda *a: ctx1.enter_context(nc.sbuf_tensor(*a))
            psb = lambda *a: ctx1.enter_context(nc.psum_tensor(*a))
            masks = sb("masks_sb", [128, 2048], F32)
            wqkv = sb("wqkv_sb", [128, DC * 1536], BF16)
            xb = sb("xb", [128, 4, D], BF16)
            hb = sb("hb", [128, 4, D], BF16)
            hT = sb("hT", [128, DC * 512], BF16)
            qT = sb("qT", [128, 2 * 2048], BF16)
            kT = sb("kT", [128, 4 * 4096], BF16)
            vext = sb("vext", [128, 4, 32, 129], BF16)
            PT = sb("PT", [128, 3072], BF16)
            ao_st = sb("ao_st", [128, 4, 512], BF16)
            rl = sb("rl", [128, 8], F32)
            lnv = sb("lnv", [128, 12], F32)
            diagR = sb("diagR", [128, 4, 128], BF16)
            ps_st = psb("ps_st", [128, 2048], F32)
            ps_pv = psb("ps_pv", [128, 1024], F32)
            ps_prepA = psb("ps_prepA", [128, 512], F32)
            ps_prepB = psb("ps_prepB", [128, 512], F32)

            def walk1(en, E):
                m = {s: 0 for s in SEMS}
                w = W(en, E, S, m)

                w.run("sync", lambda e: e.dma_start(out=ident[:, :], in_=ident_in.ap()), "s_ld", 16)
                w.run("sync", lambda e: e.dma_start(out=masks[:, :], in_=masks_in.ap()), "s_ld", 16)
                w.run("sync", lambda e: e.dma_start(
                    out=biasB[:, :], in_=bass.AP(bias_in, 0, [[0, 128], [1, D]])), "s_ld", 16)
                ld_const = m["s_ld"]

                w.wait("vector", "s_ld", ld_const)
                w.wait("tensor", "s_ld", ld_const)
                if en == "vector":
                    E.memset(ones128[:, :], 1.0)
                    E.memset(cvec[:, 0:1], SCALE)
                    E.memset(cvec[:, 1:2], EXPB)
                    E.memset(cvec[:, 2:3], 1.0 / D)
                    E.memset(cvec[:, 3:4], EPS)
                    E.memset(cvec[:, 4:5], -0.5)
                    E.memset(bass.AP(vext, 128,
                                     [[4 * 32 * 129, 128], [32 * 129, 4], [129, 32], [1, 1]])
                             , 1.0).then_inc(S["v_sem"], 1)
                m["v_sem"] += 1
                v_const = m["v_sem"]

                xld = [0] * 32
                lnx_done = [0] * 32
                stat_done = [0] * 32
                rstdA_done = [0] * 32
                trp_done = {}
                tr_evac = {}
                qk_mm = {}
                qk_evac = {}
                v_mm = {}
                v_evac = {}
                st_mm = {}
                mask_done = {}
                exp_done = {}
                pv_done = {}
                aoev_done = {}
                ag_store = {}
                st_u = [0]
                ev_u = [0]
                pp = [0]
                prep_evac = {}
                wqld_box = [0]

                def gp_x(k):
                    if k >= 4:
                        w.wait("gpsimd", "v_sem", lnx_done[k - 4])
                    sx = f"x{k % 4}"
                    w.run("gpsimd", lambda e, k=k: e.dma_start(
                        out=xb[:, k % 4, :], in_=x_in.ap()[k * 128:(k + 1) * 128, :]), sx, 16)
                    xld[k] = (sx, m[sx])

                def ln_stats(k):
                    p = k % 2
                    w.wait("vector", xld[k][0], xld[k][1])
                    if k >= 2:
                        w.wait("vector", "a_sem", rstdA_done[k - 2])
                    w.run("vector", lambda e, k=k, p=p: e.reduce_sum(
                        lnv[:, 0 + p:1 + p], xb[:, k % 4, :], axis=AX.X), "v_sem")
                    w.wait("vector", "v_sem", m["v_sem"])
                    w.run("vector", lambda e, p=p: e.tensor_scalar_mul(
                        lnv[:, 2 + p:3 + p], lnv[:, 0 + p:1 + p], -1.0 / D), "v_sem")
                    stat_done[k] = m["v_sem"]

                def ln_sq(k):
                    p = k % 2
                    w.wait("scalar", "v_sem", stat_done[k])
                    if k >= 4:
                        g_, tt_ = divmod(k - 4, 4)
                        w.wait("scalar", "t_sem", trp_done[(g_, tt_, 3)])
                    w.run("scalar", lambda e, k=k, p=p: e.activation(
                        out=hb[:, k % 4, :], in_=xb[:, k % 4, :], func=AF.Square,
                        bias=lnv[:, 2 + p:3 + p], accum_out=lnv[:, 4 + p:5 + p]), "a_sem")
                    w.wait("scalar", "a_sem", m["a_sem"])
                    w.run("scalar", lambda e, p=p: e.activation(
                        out=lnv[:, 6 + p:7 + p], in_=lnv[:, 4 + p:5 + p], func=AF.Ln,
                        scale=cvec[:, 2:3], bias=cvec[:, 3:4]), "a_sem")
                    w.wait("scalar", "a_sem", m["a_sem"])
                    w.run("scalar", lambda e, p=p: e.activation(
                        out=lnv[:, 8 + p:9 + p], in_=lnv[:, 6 + p:7 + p], func=AF.Exp,
                        scale=cvec[:, 4:5]), "a_sem")
                    w.wait("scalar", "a_sem", m["a_sem"])
                    w.run("scalar", lambda e, k=k, p=p: e.activation(
                        out=diagR[:, k % 4, :], in_=ident[:, :], func=AF.Copy,
                        scale=lnv[:, 8 + p:9 + p]), "a_sem")
                    rstdA_done[k] = m["a_sem"]

                def ln_norm(k):
                    p = k % 2
                    w.wait("vector", "a_sem", rstdA_done[k])
                    if k >= 4:
                        g_, tt_ = divmod(k - 4, 4)
                        w.wait("vector", "t_sem", trp_done[(g_, tt_, 3)])
                    w.run("vector", lambda e, k=k, p=p: e.tensor_scalar_add(
                        hb[:, k % 4, :], xb[:, k % 4, :],
                        lnv[:, 2 + p:3 + p]), "v_sem")
                    lnx_done[k] = m["v_sem"]

                def ln_tile(k):
                    ln_stats(k)
                    ln_sq(k)
                    ln_norm(k)

                def prep_acquire():
                    i = pp[0]
                    pp[0] += 1
                    if i >= 2:
                        w.wait("tensor", "v_sem", prep_evac[i - 2])
                    return (ps_prepA if i % 2 == 0 else ps_prepB), i

                def trq(gg, tt, quad):
                    k = 4 * gg + tt
                    pst, i = prep_acquire()
                    w.wait("tensor", "v_sem", lnx_done[k])
                    w.wait("tensor", "a_sem", rstdA_done[k])
                    for j in range(4):
                        dcc = quad * 4 + j
                        w.run("tensor", lambda e, k=k, dcc=dcc, j=j, pst=pst: e.matmul(
                            pst[:, 128 * j:128 * j + 128],
                            hb[:, k % 4, 128 * dcc:128 * dcc + 128],
                            diagR[:, k % 4, :],
                            start=True, stop=True), "t_sem" if j == 3 else None)
                    trp_done[(gg, tt, quad)] = m["t_sem"]
                    w.wait("vector", "t_sem", trp_done[(gg, tt, quad)])
                    w.run("vector", lambda e, tt=tt, quad=quad, pst=pst: e.tensor_copy(
                        bass.AP(hT, (quad * 4) * 512 + 128 * tt,
                                [[DC * 512, 128], [512, 4], [1, 128]]),
                        bass.AP(pst, 0, [[512, 128], [128, 4], [1, 128]])),
                        "v_sem")
                    tr_evac[(gg, tt, quad)] = m["v_sem"]
                    prep_evac[i] = m["v_sem"]

                qk_cur = {}

                def qk_half(gg, cc, half):
                    if half == 0:
                        pst, i = prep_acquire()
                        qk_cur[(gg, cc)] = (pst, i)
                        w.wait("tensor", "v_sem", tr_evac[(gg, 3, 3)])
                    pst, i = qk_cur[(gg, cc)]
                    w.wait("tensor", "s_gp", wqld_box[0][8 * half + 7])
                    for dcc in range(8 * half, 8 * half + 8):
                        w.run("tensor", lambda e, cc=cc, dcc=dcc, pst=pst: e.matmul(
                            pst[:, 0:512],
                            wqkv[:, dcc * 1536 + cc * 128: dcc * 1536 + cc * 128 + 128],
                            hT[:, dcc * 512:dcc * 512 + 512],
                            start=(dcc == 0), stop=(dcc == DC - 1)),
                            "t_sem" if dcc == DC - 1 else None)
                    if half == 1:
                        qk_mm[(gg, cc)] = m["t_sem"]
                        w.wait("vector", "t_sem", qk_mm[(gg, cc)])
                        if cc < 4:
                            w.run("vector", lambda e, cc=cc, gg=gg, pst=pst: e.tensor_copy(
                                qT[:, (gg % 2) * 2048 + cc * 512:(gg % 2) * 2048 + cc * 512 + 512],
                                pst[:, 0:512]), "v_sem")
                        else:
                            w.run("vector", lambda e, cc=cc, gg=gg, pst=pst: e.tensor_copy(
                                kT[:, (cc - 4) * 4096 + gg * 512:(cc - 4) * 4096 + gg * 512 + 512],
                                pst[:, 0:512]), "v_sem")
                        qk_evac[(gg, cc)] = m["v_sem"]
                        prep_evac[i] = m["v_sem"]

                v_cur = {}

                def v_half(gg, tt, half):
                    if half == 0:
                        pst, i = prep_acquire()
                        v_cur[(gg, tt)] = (pst, i)
                        w.wait("tensor", "v_sem", tr_evac[(gg, 3, 3)])
                    pst, i = v_cur[(gg, tt)]
                    w.wait("tensor", "s_gp", wqld_box[0][8 * half + 7])
                    for dcc in range(8 * half, 8 * half + 8):
                        w.run("tensor", lambda e, tt=tt, dcc=dcc, pst=pst: e.matmul(
                            pst[:, 0:512],
                            hT[:, dcc * 512 + 128 * tt: dcc * 512 + 128 * tt + 128],
                            wqkv[:, dcc * 1536 + 1024: dcc * 1536 + 1536],
                            start=(dcc == 0), stop=(dcc == DC - 1)),
                            "t_sem" if dcc == DC - 1 else None)
                    if half == 1:
                        v_mm[(gg, tt)] = m["t_sem"]
                        jc = 4 * gg + tt
                        w.wait("vector", "t_sem", v_mm[(gg, tt)])
                        w.wait("vector", "v_sem", v_const)
                        w.run("vector", lambda e, jc=jc, pst=pst: e.tensor_copy(
                            bass.AP(vext, jc * 129,
                                    [[4 * 32 * 129, 128], [32 * 129, 4], [1, 128]]),
                            bass.AP(pst, 0, [[512, 128], [128, 4], [1, 128]])),
                            "v_sem")
                        v_evac[(gg, tt)] = m["v_sem"]
                        prep_evac[i] = m["v_sem"]

                def prep_items(gg):
                    its = [(lambda tt=tt: ln_tile(4 * gg + tt)) for tt in range(4)]
                    if gg + 1 < NG:
                        its += [(lambda tt=tt: gp_x(4 * (gg + 1) + tt))
                                for tt in range(4)]
                    for tt in range(4):
                        for quad in range(4):
                            its.append(lambda tt=tt, quad=quad: trq(gg, tt, quad))
                    for cc in range(8):
                        for half in range(2):
                            its.append(lambda cc=cc, half=half: qk_half(gg, cc, half))
                    for tt in range(4):
                        for half in range(2):
                            its.append(lambda tt=tt, half=half: v_half(gg, tt, half))
                    return its

                wq_vals = []

                def wq_chunk(dcc):
                    w.run("gpsimd", lambda e, dcc=dcc: e.dma_start(
                        out=wqkv[:, dcc * 1536:(dcc + 1) * 1536],
                        in_=wqkv_in.ap()[dcc * 128:(dcc + 1) * 128, :]), "s_gp", 16)
                    wq_vals.append(m["s_gp"])

                gp_x(0)
                gp_x(1)
                wq_chunk(0)
                wq_chunk(1)
                gp_x(2)
                gp_x(3)
                for dcc in range(2, DC):
                    wq_chunk(dcc)
                wqld_box[0] = wq_vals

                for it in prep_items(0):
                    it()

                for g in range(NG):
                    items = prep_items(g + 1) if g + 1 < NG else []

                    for half in range(2):
                        dcc = 2 * g + half
                        w.run("gpsimd", lambda e, dcc=dcc: e.dma_start(
                            out=wout_bf.ap()[128 * dcc:128 * dcc + 128, :],
                            in_=wout_in.ap()[128 * dcc:128 * dcc + 128, :]), "wob", 16)
                        w.run("gpsimd", lambda e, dcc=dcc: e.dma_start(
                            out=wn_bf.ap()[128 * dcc:128 * dcc + 128, :],
                            in_=wn_in.ap()[128 * dcc:128 * dcc + 128, :]), "g_sem", 16)

                    units_g = []
                    for h in range(4):
                        for g2 in (2 * g, 2 * g + 1):
                            nchunk = 2 * g2 + 2
                            uls = []
                            jc0 = 0
                            if nchunk % 4 == 2:
                                uls.append((0, 2))
                                jc0 = 2
                            while jc0 < nchunk:
                                uls.append((jc0, 4))
                                jc0 += 4
                            for idx, (jc0_, wd_) in enumerate(uls):
                                units_g.append(dict(
                                    h=h, g2=g2, jc0=jc0_, wd=wd_,
                                    last=(idx == len(uls) - 1)))

                    def emit_st_exp(ud):
                        h, g2, jc0, wd = ud["h"], ud["g2"], ud["jc0"], ud["wd"]
                        u = st_u[0]
                        st_u[0] += 1
                        ud["u"] = u
                        base = 1024 * (u % 2)
                        qoff = (g % 2) * 2048 + h * 512 + 256 * (g2 % 2)
                        if u >= 2:
                            w.wait("tensor", "a_sem", exp_done[u - 2])
                        for jj in range(wd):
                            jc = jc0 + jj
                            sl = base + 256 * jj
                            w.wait("tensor", "v_sem", qk_evac[(jc // 4, 4 + h)])
                            w.wait("tensor", "v_sem", qk_evac[(g, h)])
                            w.run("tensor", lambda e, sl=sl, jc=jc, h=h, qoff=qoff: e.matmul(
                                ps_st[:, sl:sl + 256],
                                kT[:, h * 4096 + jc * 128: h * 4096 + jc * 128 + 128],
                                qT[:, qoff:qoff + 256],
                                start=True, stop=True),
                                "t_sem" if jj == wd - 1 else None)
                        st_mm[u] = m["t_sem"]

                        need_mask = (jc0 + wd - 1) >= 2 * g2
                        if need_mask:
                            w.wait("vector", "t_sem", st_mm[u])
                            dds = [jc0 + jj - 2 * g2 for jj in range(wd)]
                            dds = [(jj, d_) for jj, d_ in enumerate(dds) if d_ >= 0]
                            for ii_, (jj, dd) in enumerate(dds):
                                sl = base + 256 * jj
                                w.run("vector", lambda e, sl=sl, dd=dd: e.tensor_tensor(
                                    ps_st[:, sl:sl + 256], ps_st[:, sl:sl + 256],
                                    masks[:, dd * 512:dd * 512 + 256], ALU.add),
                                    "v_sem" if ii_ == len(dds) - 1 else None)
                            mask_done[u] = m["v_sem"]

                        w.wait("scalar", "t_sem", st_mm[u])
                        if need_mask:
                            w.wait("scalar", "v_sem", mask_done[u])
                        if u >= 3:
                            w.wait("scalar", "t_sem", pv_done[u - 3])
                        w.run("scalar", lambda e, base=base, u=u, wd=wd: e.activation(
                            out=PT[:, (u % 3) * 1024:(u % 3) * 1024 + 256 * wd],
                            in_=ps_st[:, base:base + 256 * wd], func=AF.Exp,
                            scale=cvec[:, 0:1], bias=cvec[:, 1:2]), "a_sem")
                        exp_done[u] = m["a_sem"]

                    def emit_pv(ud):
                        h, g2, jc0, wd, u = ud["h"], ud["g2"], ud["jc0"], ud["wd"], ud["u"]
                        w.wait("tensor", "a_sem", exp_done[u])
                        if jc0 == 0:
                            w.wait("tensor", "v_sem", v_evac[(g, 3)])
                            uu = ev_u[0]
                            if uu > 0:
                                w.wait("tensor", "v_sem", aoev_done[uu - 1])
                            w.wait("tensor", "v_sem", v_const)
                        for jj in range(wd):
                            jc = jc0 + jj
                            for isub in range(2):
                                off = 512 * isub
                                last = (jj == wd - 1 and isub == 1)
                                w.run("tensor",
                                      lambda e, jc=jc, off=off, u=u, jj=jj, isub=isub, h=h, g2=g2: e.matmul(
                                          ps_pv[:, off:off + 129],
                                          PT[:, (u % 3) * 1024 + 256 * jj + 128 * isub:
                                             (u % 3) * 1024 + 256 * jj + 128 * isub + 128],
                                          vext[:, h, jc, :],
                                          start=(jc == 0), stop=(jc == 2 * g2 + 1)),
                                      "t_sem" if last else None)
                        pv_done[u] = m["t_sem"]

                        if ud["last"]:
                            uu = ev_u[0]
                            ev_u[0] += 1
                            rbase = 2 * (uu % 4)
                            w.wait("vector", "t_sem", pv_done[u])
                            if g > 0 and h == 0 and g2 == 2 * g:
                                w.wait("vector", "agst", ag_store[(g - 1, 3)])
                            for isub in range(2):
                                off = 512 * isub
                                isg = 2 * (g2 % 2) + isub
                                w.run("vector", lambda e, isub=isub, off=off, rbase=rbase: e.reciprocal(
                                    rl[:, rbase + isub:rbase + isub + 1],
                                    ps_pv[:, off + 128:off + 129]), "v_sem")
                                w.wait("vector", "v_sem", m["v_sem"])
                                w.run("vector", lambda e, isub=isub, off=off, h=h, isg=isg, rbase=rbase: e.tensor_scalar_mul(
                                    ao_st[:, isg, h * 128:h * 128 + 128],
                                    ps_pv[:, off:off + 128],
                                    rl[:, rbase + isub:rbase + isub + 1]), "v_sem")
                            aoev_done[uu] = m["v_sem"]

                    pend = None
                    n_items = len(items)
                    nU = len(units_g)
                    done = 0
                    for k_u, ud in enumerate(units_g):
                        emit_st_exp(ud)
                        want = (k_u + 1) * n_items // nU
                        while done < want:
                            items[done]()
                            done += 1
                        if pend is not None:
                            emit_pv(pend)
                        pend = ud
                    emit_pv(pend)
                    while done < n_items:
                        items[done]()
                        done += 1

                    for isub in range(4):
                        w.wait("sync", "v_sem", aoev_done[ev_u[0] - 1])
                        w.run("sync", lambda e, g=g, isub=isub: e.dma_start(
                            out=ag_ins[g // 2].ap()[(g % 2) * 512 + isub * 128:
                                                    (g % 2) * 512 + isub * 128 + 128, :],
                            in_=ao_st[:, isub, :]), "agst", 16)
                        ag_store[(g, isub)] = m["agst"]

                    if g % 2 == 1:
                        kch = (g - 1) // 2
                        w.wait("gpsimd", "agst", ag_store[(g, 3)])
                        w.run("gpsimd", lambda e, kch=kch: e.collective_compute(
                            "AllGather", ALU.bypass,
                            replica_groups=[list(range(N_CORES))],
                            ins=[ag_ins[kch].ap()],
                            outs=[ag_outs[kch].ap()]), "cc", 1)


                if en == "sync":
                    E.wait_ge(S["s_ld"], m["s_ld"])
                    E.wait_ge(S["agst"], m["agst"])
                    for sx in ("sg0", "sg1"):
                        if m[sx] > 0:
                            E.wait_ge(S[sx], m[sx])
                if en == "gpsimd":
                    E.wait_ge(S["s_gp"], m["s_gp"])
                    E.wait_ge(S["g_sem"], m["g_sem"])
                    for sx in ("x0", "x1", "x2", "x3", "wt0", "wt1", "wob"):
                        if m[sx] > 0:
                            E.wait_ge(S[sx], m[sx])
                if en == "tensor":
                    E.wait_ge(S["t_sem"], m["t_sem"])
                if en == "vector":
                    E.wait_ge(S["v_sem"], m["v_sem"])
                    if m["s_ve"] > 0:
                        E.wait_ge(S["s_ve"], m["s_ve"])
                if en == "scalar":
                    E.wait_ge(S["a_sem"], m["a_sem"])
                    if m["s_sc"] > 0:
                        E.wait_ge(S["s_sc"], m["s_sc"])
                w.mark(en, "phase")
                m["phase"] = 5
                saved_m[en] = m

            with nc.Block() as blk:
                blk.sync(lambda e: walk1("sync", e))
                blk.gpsimd(lambda e: walk1("gpsimd", e))
                blk.scalar(lambda e: walk1("scalar", e))
                blk.vector(lambda e: walk1("vector", e))
                blk.tensor(lambda e: walk1("tensor", e))

        with ExitStack() as ctx2:
            sb = lambda *a: ctx2.enter_context(nc.sbuf_tensor(*a))
            psb = lambda *a: ctx2.enter_context(nc.psum_tensor(*a))
            pool = sb("pool", [128, 3, D], BF16)
            aoT = sb("aoT", [128, DC * 1024], BF16)
            wblk = sb("wblk", [128, 2, DC * 512], BF16)
            x2 = sb("x2", [128, 8, D], F32)
            h2T = sb("h2T", [128, DC * 1024], BF16)
            ystg = sb("ystg", [128, 4, 512], F32)
            lnv2 = sb("lnv2", [128, 12], F32)
            wnb = sb("wnb", [128, 2, D], BF16)
            acol = sb("acol", [128, 4], F32)
            ps_ep = psb("ps_ep", [128, 2048], F32)
            ps_f = psb("ps_f", [128, 1024], F32)
            ps_t2A = psb("ps_t2A", [128, 1024], BF16)
            ps_t2B = psb("ps_t2B", [128, 1024], BF16)

            def walk2(en, E):
                m = saved_m[en]
                w = W(en, E, S, m)
                w.wait(en, "phase", 5)

                aold = {}
                trp2 = {}
                aoT_done = [0] * 8
                opmm = {}
                x2_done = {}
                stat2 = [0] * 8
                rstd2 = [0] * 8
                norm2 = [0] * 8
                x2b_done = [0] * 8
                h2tr = {}
                h2T_done = [0] * 8
                wld = {}
                ffnmm = {}
                y_done = {}
                ystore_seq = {}
                sg_cnt = [0] * DC
                ep_last = {}
                f_last = {}
                op_seq = [0]
                y_seq = [0]
                t2p = [0]
                t2_ev = {}

                idxreg = [None]
                if en == "sync":
                    idxreg[0] = E.partition_id()

                def t2_acquire():
                    i = t2p[0]
                    t2p[0] += 1
                    if i >= 2:
                        w.wait("tensor", "v_sem", t2_ev[i - 2])
                    return (ps_t2A if i % 2 == 0 else ps_t2B), i

                wt_ld = [0] * DC

                def sgn_load(oc):
                    p = oc % 2
                    if oc >= 2:
                        w.wait("gpsimd", f"sg{p}", sg_cnt[oc - 2])
                    w.run("gpsimd", lambda e, oc=oc, p=p: e.dma_start(
                        out=wnb[:, p, :],
                        in_=wn_bf.ap()[oc * 128:oc * 128 + 128, :]), f"wt{p}", 16)
                    wt_ld[oc] = m[f"wt{p}"]

                def sgn_math(oc):
                    p = oc % 2
                    w.wait("vector", f"wt{p}", wt_ld[oc])
                    w.run("vector", lambda e, p=p: e.reduce_sum(
                        acol[:, p:p + 1], wnb[:, p, :], axis=AX.X,
                        apply_absolute_value=True), "v_sem")
                    w.wait("vector", "v_sem", m["v_sem"])
                    w.run("vector", lambda e, p=p: e.tensor_scalar_mul(
                        acol[:, 2 + p:3 + p], acol[:, p:p + 1], 1.0 / D), "v_sem")
                    red_done = m["v_sem"]
                    w.wait("scalar", "v_sem", red_done)
                    w.run("scalar", lambda e, p=p: e.activation(
                        out=wnb[:, p, :], in_=wnb[:, p, :], func=AF.Sign), "a_sem")
                    w.wait("vector", "a_sem", m["a_sem"])
                    w.run("vector", lambda e, p=p: e.tensor_scalar_mul(
                        wnb[:, p, :], wnb[:, p, :], acol[:, 2 + p:3 + p]), "v_sem")
                    w.wait("gpsimd", "v_sem", m["v_sem"])
                    w.run("gpsimd", lambda e, oc=oc, p=p: e.dma_start(
                        out=sgnN.ap()[oc * 128:oc * 128 + 128, :],
                        in_=wnb[:, p, :]), f"sg{p}", 16)
                    sg_cnt[oc] = m[f"sg{p}"]

                def wblk_load_w(og):
                    free = {2: (0, 7), 3: (1, 7)}.get(og)
                    eng2 = "gpsimd" if og < 2 else "sync"
                    for eng_l, sem_l, h in (("scalar", f"ws{og % 2}", 0),
                                            (eng2, f"wv{og % 2}", 1)):
                        if free is not None:
                            w.wait(eng_l, "t_sem", opmm[free])
                        w.run(eng_l, lambda e, og=og, h=h: e.dma_start(
                            out=wblk[:, og % 2, h * 8 * 512:(h + 1) * 8 * 512],
                            in_=bass.AP(wout_bf, og * 512 + h * 8 * 128 * D,
                                        [[D, 128], [128 * D, 8], [1, 512]])), sem_l, 16)
                    wld[og] = [(f"ws{og % 2}", m[f"ws{og % 2}"]),
                               (f"wv{og % 2}", m[f"wv{og % 2}"])]

                def wblk_load_f(og):
                    i = 4 + og
                    free = opmm[(og + 2, 7)] if og < 2 else ffnmm[(og - 2, 7)]
                    w.wait("scalar", "t_sem", free)
                    for oc in range(4 * og, 4 * og + 4):
                        w.wait("scalar", f"sg{oc % 2}", sg_cnt[oc])
                    for ic in range(DC):
                        w.run("scalar", lambda e, og=og, ic=ic: e.dma_start_transpose(
                            out=wblk[:, og % 2, ic * 512:ic * 512 + 512],
                            in_=sgnN.ap()[og * 512:og * 512 + 512,
                                          ic * 128:ic * 128 + 128]), f"wu{og % 2}", 16)
                    wld[i] = [(f"wu{og % 2}", m[f"wu{og % 2}"])]

                def slab_load(tt):
                    k = tt // 2
                    b = tt % 2
                    slot = tt % 3
                    ssl = f"sl{slot}"
                    if tt >= 3:
                        w.wait("sync", "t_sem", trp2[(tt - 3, 3)])
                    w.wait("sync", "cc", k + 1)
                    if en == "sync":
                        for ci in nc.Switch(engines=[E], index=[idxreg[0]], n=N_CORES):
                            E.dma_start(
                                out=bass.AP(pool, slot * 2048,
                                            [[3 * 2048, 128], [512, 4], [1, 512]]),
                                in_=bass.AP(ag_outs[k], (4096 * b + 128 * ci) * 512,
                                            [[512, 128], [1024 * 512, 4], [1, 512]]),
                            ).then_inc(S[ssl], 16)
                    m[ssl] += 16
                    aold[tt] = (ssl, m[ssl])

                def ao_quads(tt):
                    for quad in range(4):
                        pst, i = t2_acquire()
                        w.wait("tensor", aold[tt][0], aold[tt][1])
                        for j in range(4):
                            cc = quad * 4 + j
                            w.run("tensor", lambda e, tt=tt, cc=cc, j=j, pst=pst: e.transpose(
                                pst[:, 128 * j:128 * j + 128],
                                pool[:, tt % 3, 128 * cc:128 * cc + 128],
                                ident[:, :]), "t_sem" if j == 3 else None)
                        trp2[(tt, quad)] = m["t_sem"]
                        w.wait("vector", "t_sem", trp2[(tt, quad)])
                        w.run("vector", lambda e, tt=tt, quad=quad, pst=pst: e.tensor_copy(
                            bass.AP(aoT, (quad * 4) * 1024 + 128 * tt,
                                    [[DC * 1024, 128], [1024, 4], [1, 128]]),
                            bass.AP(pst, 0, [[1024, 128], [128, 4], [1, 128]])),
                            "v_sem")
                        t2_ev[i] = m["v_sem"]
                    aoT_done[tt] = m["v_sem"]

                def ln2_stats(tt):
                    p = tt % 2
                    if tt >= 2:
                        w.wait("vector", "a_sem", rstd2[tt - 2])
                    w.wait("vector", "v_sem", x2_done[(3, tt)])
                    w.run("vector", lambda e, tt=tt, p=p: e.reduce_sum(
                        lnv2[:, 0 + p:1 + p], x2[:, tt, :], axis=AX.X), "v_sem")
                    w.wait("vector", "v_sem", m["v_sem"])
                    w.run("vector", lambda e, p=p: e.tensor_scalar_mul(
                        lnv2[:, 2 + p:3 + p], lnv2[:, 0 + p:1 + p], -1.0 / D), "v_sem")
                    stat2[tt] = m["v_sem"]

                def ln2_sq(tt):
                    p = tt % 2
                    w.wait("scalar", "v_sem", stat2[tt])
                    w.wait("scalar", "t_sem", trp2[({0: 6, 1: 7, 2: 5}[tt % 3], 3)])
                    if tt >= 3:
                        w.wait("scalar", "t_sem", h2tr[(tt - 3, 3)])
                    w.run("scalar", lambda e, tt=tt, p=p: e.activation(
                        out=pool[:, tt % 3, :], in_=x2[:, tt, :], func=AF.Square,
                        bias=lnv2[:, 2 + p:3 + p], accum_out=lnv2[:, 4 + p:5 + p]), "a_sem")
                    w.wait("scalar", "a_sem", m["a_sem"])
                    w.run("scalar", lambda e, p=p: e.activation(
                        out=lnv2[:, 6 + p:7 + p], in_=lnv2[:, 4 + p:5 + p], func=AF.Ln,
                        scale=cvec[:, 2:3], bias=cvec[:, 3:4]), "a_sem")
                    w.wait("scalar", "a_sem", m["a_sem"])
                    w.run("scalar", lambda e, p=p: e.activation(
                        out=lnv2[:, 8 + p:9 + p], in_=lnv2[:, 6 + p:7 + p], func=AF.Exp,
                        scale=cvec[:, 4:5]), "a_sem")
                    rstd2[tt] = m["a_sem"]

                def ln2_nmr_norm(tt):
                    p = tt % 2
                    w.wait("vector", "a_sem", rstd2[tt])
                    w.run("vector", lambda e, p=p: e.tensor_mul(
                        lnv2[:, 10 + p:11 + p], lnv2[:, 2 + p:3 + p], lnv2[:, 8 + p:9 + p]),
                        "v_sem")
                    w.wait("vector", "v_sem", m["v_sem"])
                    w.run("vector", lambda e, tt=tt, p=p: e.tensor_scalar(
                        pool[:, tt % 3, :], x2[:, tt, :],
                        lnv2[:, 8 + p:9 + p], lnv2[:, 10 + p:11 + p],
                        ALU.mult, ALU.add), "v_sem")
                    norm2[tt] = m["v_sem"]
                    w.run("vector", lambda e, tt=tt: e.tensor_tensor(
                        x2[:, tt, :], x2[:, tt, :], biasB[:, :], ALU.add), "v_sem")
                    x2b_done[tt] = m["v_sem"]

                def h2_quads(tt):
                    for quad in range(4):
                        pst, i = t2_acquire()
                        w.wait("tensor", "v_sem", norm2[tt])
                        for j in range(4):
                            ic = quad * 4 + j
                            w.run("tensor", lambda e, tt=tt, ic=ic, j=j, pst=pst: e.transpose(
                                pst[:, 128 * j:128 * j + 128],
                                pool[:, tt % 3, 128 * ic:128 * ic + 128],
                                ident[:, :]), "t_sem" if j == 3 else None)
                        h2tr[(tt, quad)] = m["t_sem"]
                        w.wait("vector", "t_sem", h2tr[(tt, quad)])
                        w.run("vector", lambda e, tt=tt, quad=quad, pst=pst: e.tensor_copy(
                            bass.AP(h2T, (quad * 4) * 1024 + 128 * tt,
                                    [[DC * 1024, 128], [1024, 4], [1, 128]]),
                            bass.AP(pst, 0, [[1024, 128], [128, 4], [1, 128]])),
                            "v_sem")
                        t2_ev[i] = m["v_sem"]
                    h2T_done[tt] = m["v_sem"]

                def op_tile(og, tt):
                    sl_idx = 2 * (og % 2) + tt % 2
                    sl = 512 * sl_idx
                    prev = ep_last.get(sl_idx)
                    if prev is not None:
                        w.wait("tensor", "v_sem", x2_done[prev])
                    for s_, v_ in wld[og]:
                        w.wait("tensor", s_, v_)
                    w.wait("tensor", "v_sem", aoT_done[tt])
                    for cc in range(DC):
                        w.run("tensor", lambda e, sl=sl, cc=cc, tt=tt, og=og: e.matmul(
                            ps_ep[:, sl:sl + 512],
                            aoT[:, cc * 1024 + 128 * tt: cc * 1024 + 128 * tt + 128],
                            wblk[:, og % 2, 512 * cc:512 * cc + 512],
                            start=(cc == 0), stop=(cc == DC - 1)),
                            "t_sem" if cc == DC - 1 else None)
                    opmm[(og, tt)] = m["t_sem"]
                    ep_last[sl_idx] = (og, tt)
                    w.wait("vector", "t_sem", opmm[(og, tt)])
                    w.wait("vector", f"xr{tt % 4}", xr_ld[tt])
                    w.run("vector", lambda e, sl=sl, og=og, tt=tt: e.tensor_tensor(
                        x2[:, tt, 512 * og:512 * og + 512], ps_ep[:, sl:sl + 512],
                        x2[:, tt, 512 * og:512 * og + 512], ALU.add), "v_sem")
                    x2_done[(og, tt)] = m["v_sem"]

                def ffn_tile(og, tt):
                    sl_idx = tt % 2
                    sl = 512 * sl_idx
                    prev = f_last.get(sl_idx)
                    if prev is not None:
                        w.wait("tensor", "v_sem", y_done[prev])
                    for s_, v_ in wld[4 + og]:
                        w.wait("tensor", s_, v_)
                    w.wait("tensor", "v_sem", h2T_done[tt])
                    for ic in range(DC):
                        w.run("tensor", lambda e, sl=sl, ic=ic, tt=tt, og=og: e.matmul(
                            ps_f[:, sl:sl + 512],
                            h2T[:, ic * 1024 + 128 * tt: ic * 1024 + 128 * tt + 128],
                            wblk[:, og % 2, 512 * ic:512 * ic + 512],
                            start=(ic == 0), stop=(ic == DC - 1)),
                            "t_sem" if ic == DC - 1 else None)
                    ffnmm[(og, tt)] = m["t_sem"]
                    f_last[sl_idx] = (og, tt)
                    kk = y_seq[0]
                    y_seq[0] += 1
                    w.wait("vector", "t_sem", ffnmm[(og, tt)])
                    w.wait("vector", "v_sem", x2b_done[tt])
                    if kk >= 4:
                        w.wait("vector", ystore_seq[kk - 4][0], ystore_seq[kk - 4][1])
                    w.run("vector", lambda e, sl=sl, og=og, tt=tt, kk=kk: e.tensor_tensor(
                        ystg[:, kk % 4, :], ps_f[:, sl:sl + 512],
                        x2[:, tt, 512 * og:512 * og + 512], ALU.add), "v_sem")
                    y_done[(og, tt)] = m["v_sem"]
                    sy = f"y{kk % 4}"
                    w.wait("sync", "v_sem", y_done[(og, tt)])
                    w.run("sync", lambda e, og=og, tt=tt, kk=kk: e.dma_start(
                        out=y_out.ap()[tt * 128:tt * 128 + 128, og * 512:og * 512 + 512],
                        in_=ystg[:, kk % 4, :]), sy, 16)
                    ystore_seq[kk] = (sy, m[sy])

                wblk_load_w(0)
                wblk_load_w(1)

                xr_ld = [0] * 8

                def xr_load(tt):
                    eng_x = "scalar" if tt % 2 == 0 else "sync"
                    w.run(eng_x, lambda e, tt=tt: e.dma_start(
                        out=x2[:, tt, :],
                        in_=xres_in.ap()[tt * 128:tt * 128 + 128, :]),
                        f"xr{tt % 4}", 16)
                    xr_ld[tt] = m[f"xr{tt % 4}"]

                nx_l = [0]
                nx_m = [0]

                def sgn_step():
                    if nx_m[0] < nx_l[0] and nx_m[0] < DC:
                        sgn_math(nx_m[0])
                        nx_m[0] += 1
                    if nx_l[0] < DC and nx_l[0] <= nx_m[0] + 1:
                        sgn_load(nx_l[0])
                        nx_l[0] += 1

                sgn_step()
                for tt in range(6):
                    slab_load(tt)
                    xr_load(tt)
                    ao_quads(tt)
                    sgn_step()
                    op_tile(0, tt)
                for tt in range(6):
                    sgn_step()
                    op_tile(1, tt)
                for tt in (6, 7):
                    slab_load(tt)
                    xr_load(tt)
                    ao_quads(tt)
                    sgn_step()
                    op_tile(0, tt)
                for tt in (6, 7):
                    sgn_step()
                    op_tile(1, tt)
                wblk_load_w(2)
                wblk_load_w(3)
                for tt in range(8):
                    sgn_step()
                    sgn_step()
                    op_tile(2, tt)
                for tt in range(8):
                    op_tile(3, tt)
                    ln2_stats(tt)
                    ln2_sq(tt)
                    ln2_nmr_norm(tt)
                    if tt >= 2:
                        h2_quads(tt - 2)
                h2_quads(6)
                h2_quads(7)
                wblk_load_f(0)
                wblk_load_f(1)
                for og in range(4):
                    for tt in range(8):
                        ffn_tile(og, tt)
                    if og + 2 < 4:
                        wblk_load_f(og + 2)

                if en == "sync":
                    for sx in ("s_ld", "sl0", "sl1", "sl2",
                               "wv0", "wv1", "y0", "y1", "y2", "y3",
                               "xr0", "xr1", "xr2", "xr3"):
                        if m[sx] > 0:
                            E.wait_ge(S[sx], m[sx])
                if en == "gpsimd":
                    for sx in ("wt0", "wt1", "sg0", "sg1"):
                        if m[sx] > 0:
                            E.wait_ge(S[sx], m[sx])
                if en == "scalar":
                    for sx in ("ws0", "ws1", "wu0", "wu1"):
                        if m[sx] > 0:
                            E.wait_ge(S[sx], m[sx])

            with nc.Block() as blk2:
                blk2.sync(lambda e: walk2("sync", e))
                blk2.gpsimd(lambda e: walk2("gpsimd", e))
                blk2.scalar(lambda e: walk2("scalar", e))
                blk2.vector(lambda e: walk2("vector", e))
                blk2.tensor(lambda e: walk2("tensor", e))

    return nc


_NC_CACHE = None


def _get_nc():
    global _NC_CACHE
    if _NC_CACHE is None:
        _NC_CACHE = build_nc()
    return _NC_CACHE


def kernel(L0_attn_x, L0_attn_w_qkv, L0_attn_w_out, L0_ffn_weight, L0_ffn_bias):
    x = np.ascontiguousarray(np.asarray(L0_attn_x, dtype=np.float32))
    wqkv = np.asarray(L0_attn_w_qkv, dtype=np.float32)
    wout = np.ascontiguousarray(np.asarray(L0_attn_w_out, dtype=np.float32))
    wn = np.ascontiguousarray(np.asarray(L0_ffn_weight, dtype=np.float32))
    bias = np.ascontiguousarray(np.asarray(L0_ffn_bias, dtype=np.float32).reshape(1, D))
    ident = np.eye(128, dtype=BFNP)

    jj = np.arange(128)[:, None]
    ii = np.arange(512)[None, :]
    masks = np.concatenate(
        [np.where(ii >= jj + 128 * dd, 0.0, NEG).astype(np.float32) for dd in range(4)],
        axis=1)

    in_maps = []
    for c in range(N_CORES):
        b, g4 = divmod(c, 4)
        wq_sl = np.ascontiguousarray(np.concatenate(
            [wqkv[:, 512 * g4:512 * g4 + 512],
             wqkv[:, 2048 + 512 * g4:2048 + 512 * g4 + 512],
             wqkv[:, 4096 + 512 * g4:4096 + 512 * g4 + 512]], axis=1))
        xres_c = np.concatenate(
            [x[tt % 2, 1024 * (tt // 2) + 128 * c: 1024 * (tt // 2) + 128 * (c + 1)]
             for tt in range(8)], axis=0)
        in_maps.append({
            "x": np.ascontiguousarray(x[b]),
            "xres": np.ascontiguousarray(xres_c),
            "wqkv": wq_sl,
            "wout": wout,
            "wn": wn,
            "bias": bias,
            "ident": ident,
            "masks": masks,
        })

    nc = _get_nc()
    res = run_bass_kernel_spmd(nc, in_maps, core_ids=list(range(N_CORES)))

    out = np.empty((B, N, D), dtype=np.float32)
    for c in range(N_CORES):
        yc = res.results[c]["y"]
        for tt in range(8):
            k, bb = tt // 2, tt % 2
            out[bb, 1024 * k + 128 * c: 1024 * k + 128 * (c + 1)] = \
                yc[tt * 128:(tt + 1) * 128]
    return out
